# version 14
# speedup vs baseline: 1.2801x; 1.0213x over previous
import numpy as np
from contextlib import ExitStack
import concourse.bass as bass
import concourse.mybir as mybir
from concourse.bass_utils import run_bass_kernel_spmd

F32 = mybir.dt.float32
BF16 = mybir.dt.bfloat16
I32 = mybir.dt.int32
AF = mybir.ActivationFunctionType
ALU = mybir.AluOpType
AX = mybir.AxisListType

NCORES = 8
D = 1024
T_OWN = 2048
T_B = 8192
MT = 512
N_MT = 16
N_MT_OWN = 4
H, HD = 16, 64
NS, TOPK = 64, 8
DHF = 512
MLP = 4096
EPS = 1e-5


def _cols(v, nchunk):
    return np.ascontiguousarray(np.asarray(v, np.float32).reshape(nchunk, 128).T)


def make_core_inputs(c, inp):
    import ml_dtypes
    BF = ml_dtypes.bfloat16
    g = lambda k: np.asarray(inp["L0_" + k], dtype=np.float32)
    b = c // 4
    r0 = (c % 4) * T_OWN
    x = g("attn_q_x")[b]
    return {
        "x8k": np.ascontiguousarray(np.concatenate([x[r0:], x[:r0]], axis=0)),
        "wq": g("attn_wq"), "wk": g("attn_wk"), "wv": g("attn_wv"),
        "wo": g("attn_wo"), "fcw": g("attn_fc_w"), "pjw": g("attn_proj_w"),
        "w1a": np.ascontiguousarray(g("ffn_w1")[0:D]),
        "w1b": np.ascontiguousarray(g("ffn_w1")[D:]),
        "w2": g("ffn_w2"),
        "wihT": np.ascontiguousarray(g("ffn_Wih").T),
        "whhT": np.ascontiguousarray(g("ffn_Whh").T),
        "fwq": g("ffn_wq"), "fwk": g("ffn_wk"), "fwv": g("ffn_wv"),
        "wvp": g("ffn_wvp"), "fwo": g("ffn_wo"),
        "slotT": np.ascontiguousarray(g("ffn_slot_init").T),
        "slot": g("ffn_slot_init"),
        "ln1w_c": _cols(g("attn_ln1_w"), 8), "ln1b_c": _cols(g("attn_ln1_b"), 8),
        "ln2w_c": _cols(g("attn_ln2_w"), 8), "ln2b_c": _cols(g("attn_ln2_b"), 8),
        "lng_c": _cols(g("ffn_ln_g"), 8), "lnb_c": _cols(g("ffn_ln_b"), 8),
        "fcb_c": _cols(g("attn_fc_b"), 32), "pjb_c": _cols(g("attn_proj_b"), 8),
        "b1_c": _cols(g("ffn_b1"), 4),
        "fbq_c": _cols(g("ffn_bq"), 2), "fbk_c": _cols(g("ffn_bk"), 2),
        "b2r": g("ffn_b2")[None, :],
        "bq_r": g("attn_bq")[None, :], "bk_r": g("attn_bk")[None, :],
        "bv_r": g("attn_bv")[None, :], "bo_r": g("attn_bo")[None, :],
        "bih_r": g("ffn_bih")[None, :], "bhh_r": g("ffn_bhh")[None, :],
        "fbv_r": g("ffn_bv")[None, :], "bvp_r": g("ffn_bvp")[None, :],
        "fbo_r": g("ffn_bo")[None, :],
        "tau": g("ffn_tau").reshape(1, 1),
        "identb": np.eye(128, dtype=np.float32).astype(BF),
        "identf": np.eye(128, dtype=np.float32),
        "mask4": np.ascontiguousarray(np.tile(
            (np.arange(128)[:, None] // 16 ==
             np.arange(128)[None, :] // 16).astype(np.float32), (1, 4))),
        "onesb": np.ones((1, 512), np.float32).astype(BF),
        "onecol": np.ones((128, 1), np.float32).astype(BF),
        "magic": np.full((128, 1), 0x5f3759df, np.int32),
        "zpad": np.zeros((1024, 64), dtype=BF),
    }


def build_nc(dbg=False):
    nc = bass.Bass("TRN2", target_bir_lowering=False)

    di = lambda n, s, dt=F32: nc.dram_tensor(n, s, dt, kind="ExternalInput").ap()
    x8k = di("x8k", [T_B, D])
    wq_d = di("wq", [D, D]); wk_d = di("wk", [D, D]); wv_d = di("wv", [D, D])
    wo_d = di("wo", [D, D]); fcw_d = di("fcw", [D, MLP]); pjw_d = di("pjw", [MLP, D])
    w1a_d = di("w1a", [D, DHF]); w1b_d = di("w1b", [D, DHF]); w2_d = di("w2", [DHF, NS])
    wihT_d = di("wihT", [D, 3 * D]); whhT_d = di("whhT", [D, 3 * D])
    fwq_d = di("fwq", [D, 256]); fwk_d = di("fwk", [D, 256]); fwv_d = di("fwv", [D, D])
    wvp_d = di("wvp", [D, D]); fwo_d = di("fwo", [D, D])
    slot_d = di("slot", [NS, D]); slotT_d = di("slotT", [D, NS])
    ln1w_d = di("ln1w_c", [128, 8]); ln1b_d = di("ln1b_c", [128, 8])
    ln2w_d = di("ln2w_c", [128, 8]); ln2b_d = di("ln2b_c", [128, 8])
    lng_d = di("lng_c", [128, 8]); lnb_d = di("lnb_c", [128, 8])
    fcb_d = di("fcb_c", [128, 32]); pjb_d = di("pjb_c", [128, 8])
    b1_d = di("b1_c", [128, 4]); fbq_d = di("fbq_c", [128, 2]); fbk_d = di("fbk_c", [128, 2])
    b2r_d = di("b2r", [1, NS])
    bq_rd = di("bq_r", [1, D]); bk_rd = di("bk_r", [1, D]); bv_rd = di("bv_r", [1, D])
    bo_rd = di("bo_r", [1, D])
    bih_rd = di("bih_r", [1, 3 * D]); bhh_rd = di("bhh_r", [1, 3 * D])
    fbv_rd = di("fbv_r", [1, D]); bvp_rd = di("bvp_r", [1, D]); fbo_rd = di("fbo_r", [1, D])
    tau_d = di("tau", [1, 1])
    identb_d = di("identb", [128, 128], BF16)
    identf_d = di("identf", [128, 128])
    mask4_d = di("mask4", [128, 512])
    onesb_d = di("onesb", [1, 512], BF16)
    onecol_d = di("onecol", [128, 1], BF16)
    magic_d = di("magic", [128, 1], I32)
    zpad_d = di("zpad", [1024, 64], BF16)

    out_d = nc.dram_tensor("out", [T_OWN, D], F32, kind="ExternalOutput").ap()

    scr_qk = nc.dram_tensor("scr_qk", [32776, 128], BF16)
    scr_v = nc.dram_tensor("scr_v", [4, MT, D], BF16)
    scr_ao = nc.dram_tensor("scr_ao", [4, MT, D], BF16)
    y1scr = nc.dram_tensor("y1scr", [4, MT, D], F32)

    dbgo = {}
    if dbg:
        for n, s in [("d_xn", [128, 4096]), ("d_logits", [128, 256]),
                     ("d_alpha", [128, 1024]), ("d_si", [64, 1024]),
                     ("d_den", [64, 8]), ("d_h", [128, 2048]),
                     ("d_gh", [64, 3072]), ("d_gi", [64, 3072]),
                     ("d_snew", [64, 1024]), ("d_y", [64, 1024]),
                     ("d_sv2", [64, 1024]), ("d_fq", [128, 1024]),
                     ("d_fk", [128, 1024]), ("d_ao", [128, 1024]),
                     ("d_y1", [128, 4096]), ("d_s", [128, 4096])]:
            dbgo[n] = nc.dram_tensor(n, s, F32, kind="ExternalOutput").ap()
        dbgo["d_alphab"] = nc.dram_tensor("d_alphab", [128, 1024], BF16,
                                          kind="ExternalOutput").ap()

    with ExitStack() as ex:
        E = ex.enter_context
        sld = E(nc.semaphore("sld")); swt = E(nc.semaphore("swt"))
        sst = E(nc.semaphore("sst")); spe = E(nc.semaphore("spe"))
        sdv = E(nc.semaphore("sdv")); sac = E(nc.semaphore("sac"))
        sou = E(nc.semaphore("sou"))
        sx0 = E(nc.semaphore("sx0")); sx1 = E(nc.semaphore("sx1"))
        swA = E(nc.semaphore("swA")); swB = E(nc.semaphore("swB"))
        swc0 = E(nc.semaphore("swc0")); swc1 = E(nc.semaphore("swc1"))
        sfc0 = E(nc.semaphore("sfc0")); sfc1 = E(nc.semaphore("sfc1"))
        spj0 = E(nc.semaphore("spj0")); spj1 = E(nc.semaphore("spj1"))
        ss0 = E(nc.semaphore("ss0")); ss1 = E(nc.semaphore("ss1"))
        sao = E(nc.semaphore("sao")); sy1 = E(nc.semaphore("sy1"))
        sly0 = E(nc.semaphore("sly0")); sly1 = E(nc.semaphore("sly1"))
        sk0 = E(nc.semaphore("sk0")); sk1 = E(nc.semaphore("sk1"))
        sk2 = E(nc.semaphore("sk2")); sk3 = E(nc.semaphore("sk3"))
        sob0 = E(nc.semaphore("sob0")); sob1 = E(nc.semaphore("sob1"))
        sob2 = E(nc.semaphore("sob2")); sob3 = E(nc.semaphore("sob3"))
        sal0 = E(nc.semaphore("sal0")); sal1 = E(nc.semaphore("sal1"))
        sfq = E(nc.semaphore("sfq"))
        sx = [sx0, sx1]; swc = [swc0, swc1]; sfc = [sfc0, sfc1]
        spj = [spj0, spj1]; sly = [sly0, sly1]
        sk_ = [sk0, sk1, sk2, sk3]
        sob = [sob0, sob1, sob2, sob3]; sal = [sal0, sal1]
        X = dict(x0=0, x1=0, wA=0, wB=0, wc0=0, wc1=0, fc0=0, fc1=0,
                 pj0=0, pj1=0, s0=0, s1=0, ao=0, y1=0, ly0=0, ly1=0,
                 ss=0, k0=0, k1=0, k2=0, k3=0, ob0=0, ob1=0, ob2=0, ob3=0, al0=0,
                 al1=0, fq=0)
        ALPHA = E(nc.sbuf_tensor("ALPHA", [128, 1024], BF16))
        ALPHAT = E(nc.sbuf_tensor("ALPHAT", [64, 2048], BF16))
        SV2 = E(nc.sbuf_tensor("SV2", [64, D], BF16))
        sib = E(nc.sbuf_tensor("sib", [64, D], BF16))
        siTb = E(nc.sbuf_tensor("siTb", [128, 512], BF16))
        cIDb = E(nc.sbuf_tensor("cIDb", [128, 128], BF16))
        cONE = E(nc.sbuf_tensor("cONE", [1, 512], BF16))
        cCOL = E(nc.sbuf_tensor("cCOL", [128, 1], BF16))
        cMAG = E(nc.sbuf_tensor("cMAG", [128, 1], I32))
        LNC = E(nc.sbuf_tensor("LNC", [128, 48], F32))
        BC = E(nc.sbuf_tensor("BC", [128, 40], F32))
        B1E = E(nc.sbuf_tensor("B1E", [128, 4], F32))
        B2R = E(nc.sbuf_tensor("B2R", [1, NS], BF16))
        FBC = E(nc.sbuf_tensor("FBC", [128, 4], F32))
        ST = E(nc.sbuf_tensor("ST", [128, 96], F32))
        SCAL = E(nc.sbuf_tensor("SCAL", [128, 8], F32))
        SCB = E(nc.sbuf_tensor("SCB", [1, 8], BF16))
        JUNK = E(nc.sbuf_tensor("JUNK", [128, 1024], F32))

        C = dict(ld=0, wt=0, st=0, pe=0, dv=0, ac=0, ou=0)

        def dma(eng, o, i, sem, key, **kw):
            eng.dma_start(out=o, in_=i, **kw).then_inc(sem, 16)
            C[key] += 16

        def dmx(eng, o, i, sem, key, **kw):
            eng.dma_start(out=o, in_=i, **kw).then_inc(sem, 16)
            X[key] += 16

        def fin(inst, sem, key):
            inst.then_inc(sem, 1)
            C[key] += 1

        def finv(eng, sem, key):
            eng.drain().then_inc(sem, 1)
            C[key] += 1

        def cast_load(g, dst2d, src_t, nchunk, width, dst_off=0, src_row0=0,
                      sem=None, key=None):
            src = bass.AP(src_t.tensor, src_row0 * width,
                          [[width, 128], [width * 128, nchunk], [1, width]])
            dst = dst2d[:, dst_off:dst_off + nchunk * width].rearrange(
                "p (c w) -> p c w", c=nchunk)
            g.dma_start(out=dst, in_=src).then_inc(sem if sem is not None
                                                   else swt, 16)
            if key is not None:
                X[key] += 16
            else:
                C['wt'] += 16

        def chunked(t2d, nchunk, width):
            return t2d[:, 0:nchunk * width].rearrange("p (c w) -> p c w",
                                                      c=nchunk)

        def newton_rstd(v, mean_ap, var_ap, rstd_ap, n, rows=128):
            y = ST[0:rows, 28:28 + n]; t = ST[0:rows, 32:32 + n]
            iy = y.bitcast(I32)
            v.tensor_scalar(iy, var_ap.bitcast(I32), 1, None,
                            ALU.arith_shift_right)
            v.drain()
            v.tensor_tensor(iy, cMAG[0:rows, :].broadcast_to([rows, n]), iy,
                            ALU.subtract)
            v.drain()
            for _ in range(3):
                v.scalar_tensor_tensor(t, y, 1.0, y, ALU.mult, ALU.mult)
                v.drain()
                v.scalar_tensor_tensor(t, t, -0.5, var_ap, ALU.mult, ALU.mult)
                v.drain()
                v.tensor_scalar(t, t, 1.5, None, ALU.add)
                v.drain()
                v.tensor_tensor(y, y, t, ALU.mult)
                v.drain()
            v.tensor_copy(rstd_ap, y)
            v.drain()

        def ln_stats(v, src1024, st):
            v.drain()
            v.bn_stats(ST[:, 40:46], src1024[:, 0:512])
            v.bn_stats(ST[:, 46:52], src1024[:, 512:1024])
            v.drain()
            v.bn_aggr(ST[:, 2 * st:2 * st + 2], ST[:, 40:46])
            v.bn_aggr(ST[:, 8 + 2 * st:10 + 2 * st], ST[:, 46:52])

        def ln_combine(v, n=4):
            p1 = ST[:, 0:2 * n].rearrange("p (s two) -> p s two", two=2)
            p2 = ST[:, 8:8 + 2 * n].rearrange("p (s two) -> p s two", two=2)
            m1 = p1[:, :, 0:1].squeeze(2); v1 = p1[:, :, 1:2].squeeze(2)
            m2 = p2[:, :, 0:1].squeeze(2); v2 = p2[:, :, 1:2].squeeze(2)
            m = ST[:, 16:16 + n]; var = ST[:, 20:20 + n]; tt = ST[:, 24:24 + n]
            v.drain()
            v.tensor_add(m, m1, m2)
            v.tensor_add(var, v1, v2)
            v.tensor_sub(tt, m1, m2)
            v.drain()
            v.tensor_scalar(m, m, 0.5, None, ALU.mult)
            v.tensor_scalar(var, var, 0.5, EPS, ALU.mult, ALU.add)
            v.scalar_tensor_tensor(tt, tt, 0.25, tt, ALU.mult, ALU.mult)
            v.drain()
            v.tensor_add(var, var, tt)
            v.drain()

        with ExitStack() as exW:
            EW = exW.enter_context
            XN = EW(nc.sbuf_tensor("XN", [128, 16384], F32))
            wqb = EW(nc.sbuf_tensor("wqb", [128, 8192], BF16))
            wkb = EW(nc.sbuf_tensor("wkb", [128, 8192], BF16))
            wvb = EW(nc.sbuf_tensor("wvb", [128, 8192], BF16))

            with ExitStack() as exA:
                EA = exA.enter_context
                xnb = EA(nc.sbuf_tensor("xnb", [128, 8192], BF16))
                LOG = EA(nc.sbuf_tensor("LOG", [128, 512], F32))
                hFs = EA(nc.sbuf_tensor("hFs", [128, 2048], BF16))
                w1ab = EA(nc.sbuf_tensor("w1ab", [128, 4096], BF16))
                w2b = EA(nc.sbuf_tensor("w2b", [128, 256], BF16))
                slotTb = EA(nc.sbuf_tensor("slotTb", [128, 512], BF16))
                smT = EA(nc.sbuf_tensor("smT", [128, 8], BF16))
                xnTs = EA(nc.sbuf_tensor("xnTs", [128, 4096], BF16))
                xld = EA(nc.sbuf_tensor("xld", [128, 8192], F32))
                alsc = EA(nc.sbuf_tensor("alsc", [128, 256], BF16))
                etk = EA(nc.sbuf_tensor("etk", [128, 256], F32))
                TKS = EA(nc.sbuf_tensor("TKS", [128, 64], F32))
                E8 = EA(nc.sbuf_tensor("E8", [128, 64], F32))
                ROWB2 = EA(nc.sbuf_tensor("ROWB2", [1, 512], BF16))
                pT0 = EA(nc.psum_tensor("pT0", [128, 1024], BF16))
                pM0 = EA(nc.psum_tensor("pM0", [128, 512], F32))
                pM1 = EA(nc.psum_tensor("pM1", [128, 512], F32))
                pLa = EA(nc.psum_tensor("pLa", [128, 256], F32))
                pLb = EA(nc.psum_tensor("pLb", [128, 256], F32))
                pSI = EA(nc.psum_tensor("pSI", [64, 1024], F32))
                pDE = EA(nc.psum_tensor("pDE", [64, 8], F32))

                def xslot(mt, st):
                    k = (mt * 4 + st) % 8
                    return xld[:, k * 1024:(k + 1) * 1024]

                def xnbs(mt, st):
                    k = (mt % 2) * 4 + st
                    return xnb[:, k * 1024:(k + 1) * 1024]

                exB = ExitStack()
                w1bb = exB.enter_context(
                    nc.sbuf_tensor("w1bb", [128, 2048], BF16))

                with nc.Block() as blk:
                    @blk.sync
                    def _(s):
                        for src, dst in [(identb_d, cIDb[:]),
                                         (onesb_d, cONE[:]), (onecol_d, cCOL[:]),
                                         (magic_d, cMAG[:]),
                                         (b1_d, B1E[:]),
                                         (ln1w_d, LNC[:, 0:8]), (ln1b_d, LNC[:, 8:16]),
                                         (ln2w_d, LNC[:, 16:24]), (ln2b_d, LNC[:, 24:32]),
                                         (lng_d, LNC[:, 32:40]), (lnb_d, LNC[:, 40:48]),
                                         (fcb_d, BC[:, 0:32]), (pjb_d, BC[:, 32:40]),
                                         (fbq_d, FBC[:, 0:2]), (fbk_d, FBC[:, 2:4]),
                                         (tau_d, SCAL[0:1, 0:1])]:
                            dma(s, dst, src[:], sld, 'ld')
                        for st in range(4):
                            dmx(s, xslot(0, st), x8k[st * 128:(st + 1) * 128, :],
                                sx[0], 'x0')
                        for st in range(4):
                            r = 512 + st * 128
                            dmx(s, xslot(1, st), x8k[r:r + 128, :], sx[1], 'x1')

                    @blk.gpsimd
                    def _(g):
                        g.memset(SCAL[:, 7:8], EPS)
                        cast_load(g, w1ab, w1a_d, 8, DHF, sem=swA, key='wA')
                        cast_load(g, w1bb, w1b_d, 4, DHF, sem=swA, key='wA')
                        cast_load(g, w2b, w2_d, 4, NS, sem=swA, key='wA')
                        cast_load(g, slotTb, slotT_d, 8, NS, sem=swA, key='wA')
                        cast_load(g, wqb, wq_d, 8, D, sem=swB, key='wB')
                        cast_load(g, wkb, wk_d, 8, D, sem=swB, key='wB')
                        cast_load(g, wvb, wv_d, 8, D, sem=swB, key='wB')
                        dmx(g, B2R[:], b2r_d[:], swA, 'wA')
                        for mt4 in range(4):
                            for half in range(2):
                                po = (mt4 * 8194 + half * 4097 + 4096) * 128
                                dmx(g, bass.AP(scr_qk, po, [[1, 128]]),
                                    bass.AP(zpad_d.tensor, 0, [[1, 128]]),
                                    swB, 'wB')

                ld_p0 = C['ld']

                with nc.Block() as blk:
                    @blk.vector
                    def _(v):
                        v.wait_ge(sld, ld_p0)
                        v.wait_ge(swA, X['wA'])
                        t0 = SCAL[0:1, 0:1]
                        v.tensor_scalar(SCAL[0:1, 1:2], t0, -1.0, None, ALU.mult)
                        v.drain()
                        v.tensor_max(SCAL[0:1, 2:3], t0, SCAL[0:1, 1:2])
                        v.drain()
                        v.tensor_scalar(SCAL[0:1, 2:3], SCAL[0:1, 2:3], 0.1,
                                        None, ALU.add)
                        v.drain()
                        v.reciprocal(SCAL[0:1, 3:4], SCAL[0:1, 2:3])
                        v.drain()
                        v.tensor_copy(SCB[0:1, 0:1], SCAL[0:1, 3:4])
                        for cc in range(8):
                            v.reduce_sum(JUNK[:, cc:cc + 1],
                                         slotTb[:, cc * 64:(cc + 1) * 64],
                                         axis=AX.X)
                        v.drain()
                        i = v.tensor_scalar(smT[:, 0:8], JUNK[:, 0:8],
                                            1.0 / NS, None, ALU.mult)
                        finv(v, sdv, 'dv')

                    @blk.tensor
                    def _(t):
                        t.wait_ge(sdv, C['dv'])
                        t.matmul(pM0[:, 0:1], cONE[0:1, 0:128], SCB[0:1, 0:1],
                                 start=True, stop=True)
                        i = None
                        for cc in range(4):
                            i = t.matmul(pM1[0:1, 0:512], smT[:, cc:cc + 1],
                                         w1bb[:, cc * 512:(cc + 1) * 512],
                                         start=(cc == 0), stop=False)
                        fin(i, spe, 'pe')

                    @blk.vector
                    def _(v):
                        v.wait_ge(spe, C['pe'])
                        i = v.tensor_copy(SCAL[:, 5:6], pM0[:, 0:1])
                        finv(v, sdv, 'dv')

                with nc.Block() as blk:
                    @blk.gpsimd
                    def _(g):
                        cast_load(g, w1bb, w1b_d, 4, DHF, src_row0=512,
                                  sem=swA, key='wA')

                with nc.Block() as blk:
                    @blk.tensor
                    def _(t):
                        t.wait_ge(swA, X['wA'])
                        i = None
                        for cc in range(4):
                            i = t.matmul(pM1[0:1, 0:512], smT[:, 4 + cc:5 + cc],
                                         w1bb[:, cc * 512:(cc + 1) * 512],
                                         start=False, stop=(cc == 3))
                        fin(i, spe, 'pe')

                    @blk.vector
                    def _(v):
                        v.wait_ge(spe, C['pe'])
                        i = v.tensor_copy(ROWB2[0:1, 0:512], pM1[0:1, 0:512])
                        finv(v, sdv, 'dv')

                with nc.Block() as blk:
                    @blk.tensor
                    def _(t):
                        t.wait_ge(sdv, C['dv'])
                        i = None
                        for j in range(4):
                            i = t.transpose(pT0[:, 2 * j:2 * j + 1],
                                            ROWB2[0:1, j * 128:(j + 1) * 128],
                                            cIDb[0:1, 0:1])
                        fin(i, spe, 'pe')

                    @blk.scalar
                    def _(a):
                        a.wait_ge(spe, C['pe'])
                        i = a.activation(
                            JUNK[:, 0:4],
                            pT0[:, 0:8].rearrange(
                                "p (a two) -> p a two", two=2)[:, :, 0:1].squeeze(2),
                            AF.Identity)
                        finv(a, sac, 'ac')

                    @blk.vector
                    def _(v):
                        v.wait_ge(sac, C['ac'])
                        i = v.tensor_add(B1E[:], B1E[:], JUNK[:, 0:4])
                        finv(v, sdv, 'dv')

                exB.close()

                dv0 = C['dv']; pe0 = C['pe']; ac0 = C['ac']; ld_s = C['ld']
                own_f = lambda m: m < N_MT_OWN
                VB = []; TB = []; AB = []
                v_c = 1
                t_c = 0
                a_c = 1
                for m in range(N_MT):
                    VB.append(v_c); TB.append(t_c); AB.append(a_c)
                    v_c += ((1 if m + 1 < N_MT else 0) + 4
                            + (2 if m >= 1 else 0))
                    t_c += 9 + (0 if m == 0 else
                                1 + (1 if own_f(m - 1) else 0))
                    a_c += ((0 if m == 0 else 1)
                            + (1 if m + 1 < N_MT else 0)
                            + (4 if own_f(m) else 0) + 9
                            + (1 if m >= 1 and own_f(m - 1) else 0))
                VB.append(v_c); TB.append(t_c); AB.append(a_c)

                v_aggr = lambda m: dv0 + (1 if m == 0
                                          else VB[m - 1] + 1)
                v_apply = lambda m, st: (dv0 + VB[m]
                                         + (1 if m + 1 < N_MT else 0)
                                         + st + 1)

                def v_tailA(m):
                    if m + 1 < N_MT:
                        return (dv0 + VB[m + 1]
                                + (1 if m + 2 < N_MT else 0) + 5)
                    return dv0 + VB[N_MT] + 1

                def v_tailB(m):
                    return v_tailA(m) + 1

                t_transp = lambda m, st: pe0 + TB[m] + st + 1
                t_h = lambda m, ch: pe0 + TB[m] + 4 + ch + 1
                t_logit = lambda m: pe0 + TB[m] + 9

                def t_si(m):
                    return pe0 + (TB[m + 1] + 10 if m + 1 < N_MT
                                  else TB[N_MT] + 1)

                t_alphat = lambda m: pe0 + TB[m + 1] + 11
                aoff = lambda m: ((0 if m == 0 else 1)
                                  + (1 if m + 1 < N_MT else 0)
                                  + (4 if own_f(m) else 0))
                a_sqrt = lambda m: ac0 + (1 if m == 0 else
                                          AB[m - 1]
                                          + (0 if m == 1 else 1) + 1)
                a_copy = lambda m, st: (ac0 + AB[m]
                                        + (0 if m == 0 else 1)
                                        + (1 if m + 1 < N_MT else 0)
                                        + st + 1)
                a_evac = lambda m, st: ac0 + AB[m] + aoff(m) + st + 1
                a_gelu = lambda m, ch: ac0 + AB[m] + aoff(m) + 4 + ch + 1
                a_log = lambda m: ac0 + AB[m] + aoff(m) + 9
                a_alphat = lambda m: a_log(m + 1) + 1

                def a_exp(m):
                    return ac0 + (AB[m + 1] + 2 if m + 1 < N_MT
                                  else AB[N_MT] + 1)

                P_ = lambda m: 16 + 8 * (m % 2)
                SD_ = lambda m: 32 if m % 2 == 0 else 88
                RD_ = lambda m: 36 if m % 2 == 0 else 92

                def x_wait(mt):
                    return 64 * (mt // 2 + 1)

                with nc.Block() as blk:
                    @blk.sync
                    def _(s):
                        for m in range(2, N_MT):
                            s.wait_ge(sdv, v_apply(m - 2, 3))
                            for st in range(4):
                                r = m * 512 + st * 128
                                dmx(s, xslot(m, st), x8k[r:r + 128, :],
                                    sx[m % 2], 'x%d' % (m % 2))

                    @blk.vector
                    def _(v):
                        SB = (0, 40, 64, 76)

                        def emit_stats(mt):
                            v.wait_ge(sx[mt % 2], x_wait(mt))
                            v.drain()
                            for st in range(4):
                                v.bn_stats(ST[:, SB[st]:SB[st] + 6],
                                           xslot(mt, st)[:, 0:512])
                                v.bn_stats(ST[:, SB[st] + 6:SB[st] + 12],
                                           xslot(mt, st)[:, 512:1024])
                            v.drain()
                            pb = P_(mt)
                            for st in range(4):
                                i = v.bn_aggr(ST[:, pb + 2 * st:
                                                 pb + 2 * st + 2],
                                              ST[:, SB[st]:SB[st] + 12])
                            finv(v, sdv, 'dv')

                        def emit_apply(mt):
                            own = own_f(mt)
                            rd = RD_(mt); pb = P_(mt)
                            v.reciprocal(ST[:, rd:rd + 4],
                                         ST[:, SD_(mt):SD_(mt) + 4])
                            if mt >= 2 and not own:
                                v.wait_ge(spe, t_si(mt - 2))
                            v.drain()
                            for st in range(4):
                                xs = xslot(mt, st)
                                if own:
                                    dstx = XN[:, (mt * 4 + st) * 1024:
                                              (mt * 4 + st + 1) * 1024]
                                else:
                                    dstx = xnbs(mt, st)
                                i = v.tensor_scalar(
                                    dstx, xs,
                                    ST[:, pb + 2 * st:pb + 2 * st + 1],
                                    ST[:, rd + st:rd + st + 1],
                                    ALU.subtract, ALU.mult)
                                finv(v, sdv, 'dv')

                        def emit_tail(mt):
                            own = own_f(mt)
                            par = mt % 2
                            LOGp = LOG[:, par * 256:(par + 1) * 256]
                            TKSp = TKS[:, par * 32:(par + 1) * 32]
                            v.wait_ge(sac, a_log(mt))
                            v.drain()
                            for st in range(4):
                                v.max(TKSp[:, st * 8:(st + 1) * 8],
                                      LOGp[:, st * 64:(st + 1) * 64])
                            v.drain()
                            t8 = TKSp.rearrange("p (s e) -> p s e", e=8)
                            m1 = t8[:, :, 0:1].squeeze(2)
                            thr = t8[:, :, 7:8].squeeze(2)
                            bofs = 52 + 4 * par
                            v.tensor_tensor(
                                ST[:, bofs:bofs + 4], m1,
                                SCAL[:, 5:6].broadcast_to([128, 4]),
                                ALU.mult)
                            v.drain()
                            i = v.tensor_scalar(ST[:, bofs:bofs + 4],
                                                ST[:, bofs:bofs + 4],
                                                -1.0, None, ALU.mult)
                            finv(v, sdv, 'dv')
                            for st in range(4):
                                v.tensor_scalar(
                                    JUNK[:, st * 64:(st + 1) * 64],
                                    LOGp[:, st * 64:(st + 1) * 64],
                                    thr[:, st:st + 1], None, ALU.is_ge)
                            v.wait_ge(sac, a_exp(mt))
                            v.drain()
                            v.tensor_reduce(
                                ST[:, 60:64],
                                E8[:, par * 32:(par + 1) * 32].rearrange(
                                    "p (s e) -> p s e", e=8),
                                AX.X, ALU.add)
                            v.drain()
                            v.reciprocal(ST[:, 60:64], ST[:, 60:64])
                            if mt >= 1 and not own:
                                v.wait_ge(spe, t_si(mt - 1))
                            v.drain()
                            dst = (ALPHA[:, mt * 256:(mt + 1) * 256]
                                   if own else alsc)
                            ep = etk[:, 0:256]
                            for st in range(4):
                                i = v.scalar_tensor_tensor(
                                    dst[:, st * 64:(st + 1) * 64],
                                    ep[:, st * 64:(st + 1) * 64],
                                    ST[:, 60 + st:61 + st],
                                    JUNK[:, st * 64:(st + 1) * 64],
                                    ALU.mult, ALU.mult)
                            finv(v, sdv, 'dv')

                        emit_stats(0)
                        for mt in range(N_MT):
                            v.wait_ge(sac, a_sqrt(mt))
                            if mt + 1 < N_MT:
                                emit_stats(mt + 1)
                            emit_apply(mt)
                            if mt >= 1:
                                emit_tail(mt - 1)
                        emit_tail(N_MT - 1)

                    @blk.tensor
                    def _(t):
                        def emit_si(mt):
                            t.wait_ge(sdv, v_tailB(mt))
                            src = (ALPHA[:, mt * 256:(mt + 1) * 256]
                                   if own_f(mt) else alsc)
                            i = None
                            for st in range(4):
                                al = src[:, st * 64:(st + 1) * 64]
                                xb = xnbs(mt, st)
                                first = (mt == 0 and st == 0)
                                last = (mt == N_MT - 1 and st == 3)
                                t.matmul(pSI[:, 0:512], al, xb[:, 0:512],
                                         start=first, stop=last)
                                t.matmul(pSI[:, 512:1024], al,
                                         xb[:, 512:1024],
                                         start=first, stop=last)
                                i = t.matmul(pDE[:, 0:1], al, cCOL[:],
                                             start=first, stop=last)
                            fin(i, spe, 'pe')
                            if own_f(mt):
                                t.wait_ge(sac, a_evac(mt + 1, 3))
                                i = None
                                for st in range(4):
                                    i = t.transpose(
                                        pT0[0:64, st * 128:(st + 1) * 128],
                                        src[:, st * 64:(st + 1) * 64],
                                        cIDb[:])
                                fin(i, spe, 'pe')

                        tbank = (pT0, pM0.bitcast(BF16),
                                 pM1.bitcast(BF16), pT0)
                        for mt in range(N_MT):
                            own = own_f(mt)
                            par = mt % 2
                            for st in range(4):
                                if own:
                                    t.wait_ge(sac, a_copy(mt, st))
                                else:
                                    t.wait_ge(sdv, v_apply(mt, st))
                                if st == 0:
                                    if mt >= 1:
                                        t.wait_ge(sac, a_evac(mt - 1, 3))
                                    if mt >= 2 and own_f(mt - 2):
                                        t.wait_ge(sac,
                                                  a_alphat(mt - 2))
                                elif st == 3:
                                    t.wait_ge(sac, a_evac(mt, 0))
                                elif mt >= 1:
                                    t.wait_ge(sac,
                                              a_gelu(mt - 1, st + 1))
                                dstb = tbank[st]
                                xb = xnbs(mt, st)
                                i = None
                                for cc in range(8):
                                    i = t.transpose(
                                        dstb[:, cc * 128:(cc + 1) * 128],
                                        xb[:, cc * 128:(cc + 1) * 128],
                                        cIDb[:])
                                fin(i, spe, 'pe')
                            t.wait_ge(sac, a_evac(mt, 3))
                            for ch in range(4):
                                dstp = pM0 if ch % 2 == 0 else pM1
                                if ch >= 2:
                                    t.wait_ge(sac, a_gelu(mt, ch - 2))
                                elif mt >= 1:
                                    t.wait_ge(sac, a_gelu(mt - 1, ch + 2))
                                i = None
                                for cc in range(8):
                                    i = t.matmul(
                                        dstp[:, 0:512],
                                        w1ab[:, cc * 512 + ch * 128:
                                             cc * 512 + (ch + 1) * 128],
                                        xnTs[:, cc * 512:(cc + 1) * 512],
                                        start=(cc == 0), stop=(cc == 7))
                                fin(i, spe, 'pe')
                            t.wait_ge(sac, a_gelu(mt, 3))
                            if mt >= 2:
                                t.wait_ge(sac, a_log(mt - 2))
                            pL = pLa if par == 0 else pLb
                            i = None
                            for st in range(4):
                                for cc in range(4):
                                    t.matmul(
                                        pL[:, st * 64:(st + 1) * 64],
                                        hFs[:, cc * 512 + st * 128:
                                            cc * 512 + (st + 1) * 128],
                                        w2b[:, cc * 64:(cc + 1) * 64],
                                        start=(cc == 0), stop=False)
                                i = t.matmul(pL[:, st * 64:(st + 1) * 64],
                                             cONE[0:1, 0:128],
                                             B2R[0:1, 0:64],
                                             start=False, stop=True)
                            fin(i, spe, 'pe')
                            if mt >= 1:
                                emit_si(mt - 1)
                        emit_si(N_MT - 1)

                    @blk.scalar
                    def _(a):
                        def emit_exp(m):
                            par = m % 2
                            bofs = 52 + 4 * par
                            a.wait_ge(sdv, v_tailA(m))
                            for st in range(4):
                                a.activation(
                                    E8[:, par * 32 + st * 8:
                                       par * 32 + (st + 1) * 8],
                                    TKS[:, par * 32 + st * 8:
                                        par * 32 + (st + 1) * 8],
                                    AF.Exp,
                                    bias=ST[:, bofs + st:bofs + st + 1],
                                    scale=SCAL[:, 5:6])
                            for st in range(4):
                                i = a.activation(
                                    etk[:, st * 64:(st + 1) * 64],
                                    LOG[:, par * 256 + st * 64:
                                        par * 256 + (st + 1) * 64],
                                    AF.Exp,
                                    bias=ST[:, bofs + st:bofs + st + 1],
                                    scale=SCAL[:, 5:6])
                            finv(a, sac, 'ac')

                        def emit_sqrt(m):
                            a.wait_ge(sdv, v_aggr(m))
                            pb = P_(m); sd = SD_(m)
                            i = a.activation(
                                ST[:, sd:sd + 4],
                                ST[:, pb + 1:pb + 9].rearrange(
                                    "p (s two) -> p s two",
                                    two=2)[:, :, 0:1].squeeze(2),
                                AF.Sqrt, bias=SCAL[:, 7:8])
                            finv(a, sac, 'ac')

                        emit_sqrt(0)
                        for mt in range(N_MT):
                            own = own_f(mt)
                            par = mt % 2
                            if mt >= 1:
                                emit_exp(mt - 1)
                            if mt + 1 < N_MT:
                                emit_sqrt(mt + 1)
                            if own:
                                for st in range(4):
                                    a.wait_ge(sdv, v_apply(mt, st))
                                    if st == 0 and mt >= 2:
                                        a.wait_ge(spe, t_si(mt - 2))
                                    i = a.activation(
                                        xnbs(mt, st),
                                        XN[:, (mt * 4 + st) * 1024:
                                           (mt * 4 + st + 1) * 1024],
                                        AF.Identity)
                                    finv(a, sac, 'ac')
                            for st in range(4):
                                a.wait_ge(spe, t_transp(mt, st))
                                srcb = (pT0, pM0.bitcast(BF16),
                                        pM1.bitcast(BF16), pT0)[st]
                                i = a.activation(
                                    chunked(xnTs, 8, 512)[
                                        :, :, st * 128:(st + 1) * 128],
                                    chunked(srcb, 8, 128),
                                    AF.Identity)
                                finv(a, sac, 'ac')
                            for ch in range(4):
                                a.wait_ge(spe, t_h(mt, ch))
                                srcp = pM0 if ch % 2 == 0 else pM1
                                i = a.activation(
                                    hFs[:, ch * 512:(ch + 1) * 512],
                                    srcp[:, 0:512], AF.Gelu,
                                    bias=B1E[:, ch:ch + 1])
                                finv(a, sac, 'ac')
                            a.wait_ge(spe, t_logit(mt))
                            if mt >= 2:
                                a.wait_ge(sdv, v_tailB(mt - 2))
                            pL = pLa if par == 0 else pLb
                            i = a.activation(
                                LOG[:, par * 256:(par + 1) * 256],
                                pL[:, :], AF.Identity)
                            finv(a, sac, 'ac')
                            if mt >= 1 and own_f(mt - 1):
                                a.wait_ge(spe, t_alphat(mt - 1))
                                i = a.activation(
                                    ALPHAT[:, (mt - 1) * 512:mt * 512],
                                    pT0[0:64, 0:512], AF.Identity)
                                finv(a, sac, 'ac')
                        emit_exp(N_MT - 1)

                assert C['dv'] == dv0 + VB[N_MT] + 2, (C['dv'],
                                                       dv0 + VB[N_MT] + 2)
                assert C['pe'] == pe0 + TB[N_MT] + 1, (C['pe'],
                                                       pe0 + TB[N_MT] + 1)
                assert C['ac'] == ac0 + AB[N_MT] + 1, (C['ac'],
                                                       ac0 + AB[N_MT] + 1)

                with nc.Block() as blk:
                    @blk.vector
                    def _(v):
                        v.tensor_scalar(ST[0:64, 0:1], pDE[:, 0:1], 1e-8,
                                        None, ALU.add)
                        v.drain()
                        v.reciprocal(ST[0:64, 1:2], ST[0:64, 0:1])
                        v.drain()
                        i = v.tensor_scalar(sib[:], pSI[:, :], ST[0:64, 1:2],
                                            None, ALU.mult)
                        finv(v, sdv, 'dv')

                    @blk.tensor
                    def _(t):
                        t.wait_ge(sdv, C['dv'])
                        i = None
                        for cc in range(8):
                            i = t.transpose(pT0[:, cc * 64:(cc + 1) * 64],
                                            sib[:, cc * 128:(cc + 1) * 128],
                                            cIDb[0:64, 0:64])
                        fin(i, spe, 'pe')

                    @blk.vector
                    def _(v):
                        v.wait_ge(spe, C['pe'])
                        i = v.tensor_copy(siTb[:], pT0[:, 0:512])
                        finv(v, sdv, 'dv')

                if dbg:
                    with nc.Block() as blk:
                        @blk.vector
                        def _(v):
                            v.tensor_copy(JUNK[:, 0:1024], ALPHA[:])
                            i = v.tensor_copy(ST[0:64, 0:8], pDE[:, :])
                            finv(v, sdv, 'dv')

                        @blk.sync
                        def _(s):
                            s.wait_ge(sdv, C['dv'])
                            dma(s, dbgo["d_xn"], XN[:, 0:4096], sou, 'ou')
                            dma(s, dbgo["d_logits"], LOG[:, 256:512], sou, 'ou')
                            dma(s, dbgo["d_alpha"], JUNK[:, 0:1024], sou, 'ou')
                            dma(s, dbgo["d_alphab"], ALPHA[:], sou, 'ou')
                            dma(s, dbgo["d_den"], ST[0:64, 0:8], sou, 'ou')

                    with nc.Block() as blk:
                        @blk.vector
                        def _(v):
                            i = v.tensor_copy(JUNK[0:64, 0:1024], sib[:])
                            finv(v, sdv, 'dv')

                        @blk.sync
                        def _(s):
                            s.wait_ge(sdv, C['dv'])
                            dma(s, dbgo["d_si"], JUNK[0:64, 0:1024], sou, 'ou')


            with ExitStack() as exB3:
                EB = exB3.enter_context
                ROWB3 = EB(nc.sbuf_tensor("ROWB3", [1, 9216], BF16))
                slotb = EB(nc.sbuf_tensor("slotb", [64, 1024], F32))
                slotTb2 = EB(nc.sbuf_tensor("slotTb2", [128, 512], BF16))
                whb = EB(nc.sbuf_tensor("whb", [128, 6144], BF16))
                ghb = EB(nc.sbuf_tensor("ghb", [64, 3072], BF16))
                fwqb = EB(nc.sbuf_tensor("fwqb", [128, 2048], BF16))
                fwkb = EB(nc.sbuf_tensor("fwkb", [128, 2048], BF16))
                rb = EB(nc.sbuf_tensor("rb", [64, 1024], F32))
                nb = EB(nc.sbuf_tensor("nb", [64, 1024], F32))
                sc3 = EB(nc.sbuf_tensor("sc3", [64, 1024], F32))
                snew = EB(nc.sbuf_tensor("snew", [64, 1024], F32))
                snb = EB(nc.sbuf_tensor("snb", [64, 1024], BF16))
                snT = EB(nc.sbuf_tensor("snT", [128, 512], BF16))
                qkT = EB(nc.sbuf_tensor("qkT", [128, 256], BF16))
                atS = EB(nc.sbuf_tensor("atS", [64, 64], BF16))
                atT = EB(nc.sbuf_tensor("atT", [64, 64], BF16))
                vfb = EB(nc.sbuf_tensor("vfb", [64, 1024], BF16))
                yhb = EB(nc.sbuf_tensor("yhb", [64, 1024], BF16))
                yhT = EB(nc.sbuf_tensor("yhT", [128, 512], BF16))
                svb = EB(nc.sbuf_tensor("svb", [64, 1024], BF16))
                svT = EB(nc.sbuf_tensor("svT", [128, 512], BF16))
                pGH = EB(nc.psum_tensor("pGH", [64, 3072], F32))
                pS3 = EB(nc.psum_tensor("pS3", [128, 512], F32))
                pT3 = EB(nc.psum_tensor("pT3", [128, 512], BF16))
                wslot = lambda c: whb[:, (c % 2) * 3072:(c % 2) * 3072 + 3072]
                wslot1k = lambda c: whb[:, (c % 2) * 1024:(c % 2) * 1024 + 1024]

                with nc.Block() as blk:
                    @blk.sync
                    def _(s):
                        dma(s, slotb[:], slot_d[:], sld, 'ld')

                    @blk.gpsimd
                    def _(g):
                        dmx(g, ROWB3[0:1, 0:3072], bih_rd[:], swA, 'wA')
                        dmx(g, ROWB3[0:1, 3072:6144], bhh_rd[:], swA, 'wA')
                        dmx(g, ROWB3[0:1, 6144:7168], fbv_rd[:], swA, 'wA')
                        dmx(g, ROWB3[0:1, 7168:8192], bvp_rd[:], swA, 'wA')
                        dmx(g, ROWB3[0:1, 8192:9216], fbo_rd[:], swA, 'wA')
                        cast_load(g, slotTb2, slotT_d, 8, NS, sem=swA, key='wA')
                        cast_load(g, fwqb, fwq_d, 8, 256, sem=swA, key='wA')
                        cast_load(g, fwkb, fwk_d, 8, 256, sem=swA, key='wA')
                        cast_load(g, wslot(0), whhT_d, 1, 3072, src_row0=0,
                                  sem=swc[0], key='wc0')
                        cast_load(g, wslot(1), whhT_d, 1, 3072, src_row0=128,
                                  sem=swc[1], key='wc1')

                ldB0 = C['ld']
                wA_B0 = X['wA']

                with nc.Block() as blk:
                    peS = C['pe']
                    wcB = [X['wc0'], X['wc1']]

                    @blk.gpsimd
                    def _(g):
                        for c in range(2, 8):
                            g.wait_ge(spe, peS + c - 1)
                            cast_load(g, wslot(c), whhT_d, 1, 3072,
                                      src_row0=c * 128,
                                      sem=swc[c % 2], key='wc%d' % (c % 2))

                    @blk.tensor
                    def _(t):
                        t.wait_ge(sld, ldB0)
                        t.wait_ge(swA, wA_B0)
                        for c in range(8):
                            t.wait_ge(swc[c % 2],
                                      wcB[c % 2] + 16 * (c // 2 + 1) - 16)
                            i = None
                            for jc in range(6):
                                i = t.matmul(
                                    pGH[:, jc * 512:(jc + 1) * 512],
                                    slotTb2[:, c * 64:(c + 1) * 64],
                                    wslot(c)[:, jc * 512:(jc + 1) * 512],
                                    start=(c == 0), stop=False)
                            fin(i, spe, 'pe')
                        i = None
                        for jc in range(6):
                            i = t.matmul(pGH[:, jc * 512:(jc + 1) * 512],
                                         cONE[0:1, 0:64],
                                         ROWB3[0:1, 3072 + jc * 512:
                                               3072 + (jc + 1) * 512],
                                         start=False, stop=True)
                        fin(i, spe, 'pe')

                    @blk.vector
                    def _(v):
                        v.wait_ge(spe, C['pe'])
                        i = v.tensor_copy(ghb[:], pGH[:, :])
                        finv(v, sdv, 'dv')

                with nc.Block() as blk:
                    peS = C['pe']
                    wcB2 = [X['wc0'], X['wc1']]

                    @blk.gpsimd
                    def _(g):
                        cast_load(g, wslot(0), wihT_d, 1, 3072, src_row0=0,
                                  sem=swc[0], key='wc0')
                        cast_load(g, wslot(1), wihT_d, 1, 3072, src_row0=128,
                                  sem=swc[1], key='wc1')
                        for c in range(2, 8):
                            g.wait_ge(spe, peS + c - 1)
                            cast_load(g, wslot(c), wihT_d, 1, 3072,
                                      src_row0=c * 128,
                                      sem=swc[c % 2], key='wc%d' % (c % 2))

                    @blk.tensor
                    def _(t):
                        for c in range(8):
                            t.wait_ge(swc[c % 2],
                                      wcB2[c % 2] + 16 * (c // 2 + 1))
                            i = None
                            for jc in range(6):
                                i = t.matmul(
                                    pGH[:, jc * 512:(jc + 1) * 512],
                                    siTb[:, c * 64:(c + 1) * 64],
                                    wslot(c)[:, jc * 512:(jc + 1) * 512],
                                    start=(c == 0), stop=False)
                            fin(i, spe, 'pe')
                        i = None
                        for jc in range(6):
                            i = t.matmul(pGH[:, jc * 512:(jc + 1) * 512],
                                         cONE[0:1, 0:64],
                                         ROWB3[0:1, jc * 512:(jc + 1) * 512],
                                         start=False, stop=True)
                        fin(i, spe, 'pe')

                with nc.Block() as blk:
                    dvS = C['dv']; acS = C['ac']

                    @blk.vector
                    def _(v):
                        i = v.tensor_add(sc3[:], pGH[:, 0:1024], ghb[:, 0:1024])
                        finv(v, sdv, 'dv')
                        v.wait_ge(sac, acS + 1)
                        v.tensor_tensor(JUNK[0:64, 0:1024], rb[:],
                                        ghb[:, 2048:3072], ALU.mult)
                        v.drain()
                        i = v.tensor_add(JUNK[0:64, 0:1024],
                                         JUNK[0:64, 0:1024], pGH[:, 2048:3072])
                        finv(v, sdv, 'dv')
                        i = v.tensor_add(sc3[:], pGH[:, 1024:2048],
                                         ghb[:, 1024:2048])
                        finv(v, sdv, 'dv')
                        v.wait_ge(sac, acS + 3)
                        v.tensor_sub(sc3[:], slotb[:], nb[:])
                        v.drain()
                        v.tensor_tensor(sc3[:], sc3[:], rb[:], ALU.mult)
                        v.drain()
                        v.tensor_add(snew[:], sc3[:], nb[:])
                        v.drain()
                        i = v.tensor_copy(snb[:], snew[:])
                        finv(v, sdv, 'dv')

                    @blk.scalar
                    def _(a):
                        a.wait_ge(sdv, dvS + 1)
                        i = a.activation(rb[:], sc3[:], AF.Sigmoid)
                        finv(a, sac, 'ac')
                        a.wait_ge(sdv, dvS + 2)
                        i = a.activation(nb[:], JUNK[0:64, 0:1024], AF.Tanh)
                        finv(a, sac, 'ac')
                        a.wait_ge(sdv, dvS + 3)
                        i = a.activation(rb[:], sc3[:], AF.Sigmoid)
                        finv(a, sac, 'ac')

                with nc.Block() as blk:
                    dvS = C['dv']; peS = C['pe']; acS = C['ac']

                    @blk.tensor
                    def _(t):
                        i = None
                        for cc in range(8):
                            i = t.transpose(pT3[:, cc * 64:(cc + 1) * 64],
                                            snb[:, cc * 128:(cc + 1) * 128],
                                            cIDb[0:64, 0:64])
                        fin(i, spe, 'pe')
                        t.wait_ge(sdv, dvS + 1)
                        i = None
                        for qc in range(2):
                            for cc in range(8):
                                i = t.matmul(
                                    pS3[:, qc * 64:(qc + 1) * 64],
                                    fwqb[:, cc * 256 + qc * 128:
                                         cc * 256 + (qc + 1) * 128],
                                    snT[:, cc * 64:(cc + 1) * 64],
                                    start=(cc == 0), stop=(cc == 7))
                            for cc in range(8):
                                i = t.matmul(
                                    pS3[:, 128 + qc * 64:128 + (qc + 1) * 64],
                                    fwkb[:, cc * 256 + qc * 128:
                                         cc * 256 + (qc + 1) * 128],
                                    snT[:, cc * 64:(cc + 1) * 64],
                                    start=(cc == 0), stop=(cc == 7))
                        fin(i, spe, 'pe')
                        t.wait_ge(sac, acS + 1)
                        i = None
                        for qc in range(2):
                            i = t.matmul(pS3[0:64, 256:320],
                                         qkT[:, qc * 64:(qc + 1) * 64],
                                         qkT[:, 128 + qc * 64:128 + (qc + 1) * 64],
                                         start=(qc == 0), stop=(qc == 1))
                        fin(i, spe, 'pe')
                        t.wait_ge(sdv, dvS + 3)
                        i = t.transpose(pT3[0:64, 0:64], atS[:],
                                        cIDb[0:64, 0:64])
                        fin(i, spe, 'pe')

                    @blk.vector
                    def _(v):
                        v.wait_ge(spe, peS + 1)
                        i = v.tensor_copy(snT[:], pT3[:, :])
                        finv(v, sdv, 'dv')
                        v.wait_ge(spe, peS + 3)
                        v.reduce_max(ST[0:64, 0:1], pS3[0:64, 256:320],
                                     axis=AX.X, negate=True)
                        v.drain()
                        i = v.tensor_scalar(ST[0:64, 1:2], ST[0:64, 0:1],
                                            0.0625, None, ALU.mult)
                        finv(v, sdv, 'dv')
                        v.wait_ge(sac, acS + 2)
                        v.reduce_sum(ST[0:64, 2:3], JUNK[0:64, 0:64], axis=AX.X)
                        v.drain()
                        v.reciprocal(ST[0:64, 3:4], ST[0:64, 2:3])
                        v.drain()
                        i = v.tensor_scalar(atS[:], JUNK[0:64, 0:64],
                                            ST[0:64, 3:4], None, ALU.mult)
                        finv(v, sdv, 'dv')
                        v.wait_ge(spe, peS + 4)
                        i = v.tensor_copy(atT[:], pT3[0:64, 0:64])
                        finv(v, sdv, 'dv')

                    @blk.scalar
                    def _(a):
                        a.wait_ge(spe, peS + 2)
                        for qc in range(2):
                            a.activation(qkT[:, qc * 64:(qc + 1) * 64],
                                         pS3[:, qc * 64:(qc + 1) * 64],
                                         AF.Identity, bias=FBC[:, qc:qc + 1])
                            i = a.activation(
                                qkT[:, 128 + qc * 64:128 + (qc + 1) * 64],
                                pS3[:, 128 + qc * 64:128 + (qc + 1) * 64],
                                AF.Identity, bias=FBC[:, 2 + qc:3 + qc])
                        finv(a, sac, 'ac')
                        a.wait_ge(sdv, dvS + 2)
                        i = a.activation(JUNK[0:64, 0:64], pS3[0:64, 256:320],
                                         AF.Exp, bias=ST[0:64, 1:2],
                                         scale=0.0625)
                        finv(a, sac, 'ac')

                with nc.Block() as blk:
                    dvS = C['dv']; peS = C['pe']
                    wcB4 = [X['wc0'], X['wc1']]

                    @blk.gpsimd
                    def _(g):
                        cast_load(g, wslot1k(0), fwv_d, 1, 1024, src_row0=0,
                                  sem=swc[0], key='wc0')
                        cast_load(g, wslot1k(1), fwv_d, 1, 1024, src_row0=128,
                                  sem=swc[1], key='wc1')
                        for c in range(2, 8):
                            g.wait_ge(spe, peS + c - 1)
                            cast_load(g, wslot1k(c), fwv_d, 1, 1024,
                                      src_row0=c * 128,
                                      sem=swc[c % 2], key='wc%d' % (c % 2))

                    @blk.tensor
                    def _(t):
                        for c in range(8):
                            t.wait_ge(swc[c % 2],
                                      wcB4[c % 2] + 16 * (c // 2 + 1))
                            i = None
                            for nh in range(2):
                                i = t.matmul(
                                    pGH[:, nh * 512:(nh + 1) * 512],
                                    snT[:, c * 64:(c + 1) * 64],
                                    wslot1k(c)[:, nh * 512:(nh + 1) * 512],
                                    start=(c == 0), stop=False)
                            fin(i, spe, 'pe')
                        i = None
                        for nh in range(2):
                            i = t.matmul(pGH[:, nh * 512:(nh + 1) * 512],
                                         cONE[0:1, 0:64],
                                         ROWB3[0:1, 6144 + nh * 512:
                                               6144 + (nh + 1) * 512],
                                         start=False, stop=True)
                        fin(i, spe, 'pe')
                        t.wait_ge(sdv, dvS + 1)
                        i = None
                        for nh in range(2):
                            i = t.matmul(pGH[:, 2048 + nh * 512:
                                             2048 + (nh + 1) * 512],
                                         atT[:],
                                         vfb[:, nh * 512:(nh + 1) * 512],
                                         start=True, stop=True)
                        fin(i, spe, 'pe')
                        t.wait_ge(sdv, dvS + 3)
                        i = None
                        for cc in range(8):
                            i = t.transpose(pT3[:, cc * 64:(cc + 1) * 64],
                                            yhb[:, cc * 128:(cc + 1) * 128],
                                            cIDb[0:64, 0:64])
                        fin(i, spe, 'pe')

                    @blk.vector
                    def _(v):
                        v.wait_ge(spe, peS + 9)
                        i = v.tensor_copy(vfb[:], pGH[:, 0:1024])
                        finv(v, sdv, 'dv')
                        v.wait_ge(spe, peS + 10)
                        v.tensor_add(rb[:], pGH[:, 2048:3072], snew[:])
                        v.drain()
                        v.bn_stats(ST[0:64, 40:46], rb[:, 0:512])
                        v.bn_stats(ST[0:64, 46:52], rb[:, 512:1024])
                        v.drain()
                        v.bn_aggr(ST[0:64, 0:2], ST[0:64, 40:46])
                        i = v.bn_aggr(ST[0:64, 8:10], ST[0:64, 46:52])
                        finv(v, sdv, 'dv')
                        v.tensor_add(ST[0:64, 16:17], ST[0:64, 0:1],
                                     ST[0:64, 8:9])
                        v.tensor_add(ST[0:64, 20:21], ST[0:64, 1:2],
                                     ST[0:64, 9:10])
                        v.tensor_sub(ST[0:64, 24:25], ST[0:64, 0:1],
                                     ST[0:64, 8:9])
                        v.drain()
                        v.tensor_scalar(ST[0:64, 16:17], ST[0:64, 16:17],
                                        0.5, None, ALU.mult)
                        v.tensor_scalar(ST[0:64, 20:21], ST[0:64, 20:21],
                                        0.5, EPS, ALU.mult, ALU.add)
                        v.scalar_tensor_tensor(ST[0:64, 24:25], ST[0:64, 24:25],
                                               0.25, ST[0:64, 24:25],
                                               ALU.mult, ALU.mult)
                        v.drain()
                        v.tensor_add(ST[0:64, 20:21], ST[0:64, 20:21],
                                     ST[0:64, 24:25])
                        v.drain()
                        newton_rstd(v, ST[0:64, 16:17], ST[0:64, 20:21],
                                    ST[0:64, 36:37], 1, rows=64)
                        i = v.tensor_scalar(yhb[:], rb[:], ST[0:64, 16:17],
                                            ST[0:64, 36:37],
                                            ALU.subtract, ALU.mult)
                        finv(v, sdv, 'dv')

                with nc.Block() as blk:
                    peS = C['pe']; dvS = C['dv']
                    wcB5 = [X['wc0'], X['wc1']]

                    @blk.gpsimd
                    def _(g):
                        cast_load(g, wslot1k(0), wvp_d, 1, 1024, src_row0=0,
                                  sem=swc[0], key='wc0')
                        cast_load(g, wslot1k(1), wvp_d, 1, 1024, src_row0=128,
                                  sem=swc[1], key='wc1')
                        for c in range(2, 8):
                            g.wait_ge(spe, peS + c)
                            cast_load(g, wslot1k(c), wvp_d, 1, 1024,
                                      src_row0=c * 128,
                                      sem=swc[c % 2], key='wc%d' % (c % 2))

                    @blk.vector
                    def _(v):
                        for cc in range(8):
                            i = v.tensor_scalar(yhT[:, cc * 64:(cc + 1) * 64],
                                                pT3[:, cc * 64:(cc + 1) * 64],
                                                LNC[:, 32 + cc:33 + cc],
                                                LNC[:, 40 + cc:41 + cc],
                                                ALU.mult, ALU.add)
                        finv(v, sdv, 'dv')
                        v.wait_ge(spe, peS + 9)
                        i = v.tensor_copy(svb[:], pGH[:, 0:1024])
                        finv(v, sdv, 'dv')

                    @blk.tensor
                    def _(t):
                        t.wait_ge(sdv, dvS + 1)
                        for c in range(8):
                            t.wait_ge(swc[c % 2],
                                      wcB5[c % 2] + 16 * (c // 2 + 1))
                            i = None
                            for nh in range(2):
                                i = t.matmul(pGH[:, nh * 512:(nh + 1) * 512],
                                             yhT[:, c * 64:(c + 1) * 64],
                                             wslot1k(c)[:, nh * 512:(nh + 1) * 512],
                                             start=(c == 0), stop=False)
                            fin(i, spe, 'pe')
                        i = None
                        for nh in range(2):
                            i = t.matmul(pGH[:, nh * 512:(nh + 1) * 512],
                                         cONE[0:1, 0:64],
                                         ROWB3[0:1, 7168 + nh * 512:
                                               7168 + (nh + 1) * 512],
                                         start=False, stop=True)
                        fin(i, spe, 'pe')
                        t.wait_ge(sdv, dvS + 2)
                        i = None
                        for cc in range(8):
                            i = t.transpose(pT3[:, cc * 64:(cc + 1) * 64],
                                            svb[:, cc * 128:(cc + 1) * 128],
                                            cIDb[0:64, 0:64])
                        fin(i, spe, 'pe')

                with nc.Block() as blk:
                    peS = C['pe']; dvS = C['dv']
                    wcB6 = [X['wc0'], X['wc1']]

                    @blk.gpsimd
                    def _(g):
                        cast_load(g, wslot1k(0), fwo_d, 1, 1024, src_row0=0,
                                  sem=swc[0], key='wc0')
                        cast_load(g, wslot1k(1), fwo_d, 1, 1024, src_row0=128,
                                  sem=swc[1], key='wc1')
                        for c in range(2, 8):
                            g.wait_ge(spe, peS + c - 1)
                            cast_load(g, wslot1k(c), fwo_d, 1, 1024,
                                      src_row0=c * 128,
                                      sem=swc[c % 2], key='wc%d' % (c % 2))

                    @blk.vector
                    def _(v):
                        i = v.tensor_copy(svT[:], pT3[:, :])
                        finv(v, sdv, 'dv')

                    @blk.tensor
                    def _(t):
                        t.wait_ge(sdv, dvS + 1)
                        for c in range(8):
                            t.wait_ge(swc[c % 2],
                                      wcB6[c % 2] + 16 * (c // 2 + 1))
                            i = None
                            for nh in range(2):
                                i = t.matmul(pGH[:, 1024 + nh * 512:
                                                 1024 + (nh + 1) * 512],
                                             svT[:, c * 64:(c + 1) * 64],
                                             wslot1k(c)[:, nh * 512:(nh + 1) * 512],
                                             start=(c == 0), stop=False)
                            fin(i, spe, 'pe')
                        i = None
                        for nh in range(2):
                            i = t.matmul(pGH[:, 1024 + nh * 512:
                                             1024 + (nh + 1) * 512],
                                         cONE[0:1, 0:64],
                                         ROWB3[0:1, 8192 + nh * 512:
                                               8192 + (nh + 1) * 512],
                                         start=False, stop=True)
                        fin(i, spe, 'pe')

                    @blk.vector
                    def _(v):
                        v.wait_ge(spe, C['pe'])
                        i = v.tensor_copy(SV2[:], pGH[:, 1024:2048])
                        finv(v, sdv, 'dv')

                if dbg:
                    with nc.Block() as blk:
                        @blk.vector
                        def _(v):
                            v.tensor_copy(JUNK[0:64, 0:1024], ghb[:, 0:1024])
                            i = v.tensor_copy(sc3[:], snew[:])
                            finv(v, sdv, 'dv')

                        @blk.sync
                        def _(s):
                            s.wait_ge(sdv, C['dv'])
                            dma(s, dbgo["d_gh"][0:64, 0:1024],
                                JUNK[0:64, 0:1024], sou, 'ou')
                            dma(s, dbgo["d_snew"], snew[:], sou, 'ou')
                            dma(s, dbgo["d_y"], rb[:], sou, 'ou')

                    with nc.Block() as blk:
                        @blk.vector
                        def _(v):
                            i = v.tensor_copy(JUNK[0:64, 0:1024], SV2[:])
                            finv(v, sdv, 'dv')

                        @blk.sync
                        def _(s):
                            s.wait_ge(sdv, C['dv'])
                            dma(s, dbgo["d_sv2"], JUNK[0:64, 0:1024], sou, 'ou')


            with ExitStack() as exC:
                EC = exC.enter_context
                wob = EC(nc.sbuf_tensor("wob", [128, 8192], BF16))
                ROWA = EC(nc.sbuf_tensor("ROWA", [1, 4096], BF16))
                cMSK = EC(nc.sbuf_tensor("cMSK", [128, 512], F32))
                xhb = EC(nc.sbuf_tensor("xhb", [128, 2048], BF16))
                xhT = EC(nc.sbuf_tensor("xhT", [128, 4096], BF16))
                tmb = EC(nc.sbuf_tensor("tmb", [128, 2048], BF16))
                FQ2h = EC(nc.sbuf_tensor("FQ2h", [128, 4096], BF16))
                FK2h = EC(nc.sbuf_tensor("FK2h", [128, 4096], BF16))
                V2h = EC(nc.sbuf_tensor("V2h", [128, 2048], BF16))
                AMK = EC(nc.sbuf_tensor("AMK", [128, 1024], BF16))
                aog = EC(nc.sbuf_tensor("aog", [128, 2048], BF16))
                aoTM = EC(nc.sbuf_tensor("aoTM", [128, 2048], BF16))
                aoT = EC(nc.sbuf_tensor("aoT", [128, 1024], BF16))
                Rt = EC(nc.sbuf_tensor("Rt", [128, 512], F32))
                qkr = EC(nc.sbuf_tensor("qkr", [128, 32], F32))
                pT0c = EC(nc.psum_tensor("pT0c", [128, 1024], BF16))
                pBig0 = EC(nc.psum_tensor("pBig0", [128, 1024], F32))
                pBig1 = EC(nc.psum_tensor("pBig1", [128, 1024], F32))
                pQKa = EC(nc.psum_tensor("pQKa", [128, 16], F32))
                pQKb = EC(nc.psum_tensor("pQKb", [128, 16], F32))
                rowA = lambda i2: ROWA[0:1, i2 * 1024:(i2 + 1) * 1024]

                with nc.Block() as blk:
                    @blk.sync
                    def _(s):
                        dma(s, cMSK[:], mask4_d[:], sld, 'ld')

                    @blk.gpsimd
                    def _(g):
                        cast_load(g, wob, wo_d, 8, D, sem=swB, key='wB')
                        for srcd, i2 in [(bq_rd, 0), (bk_rd, 1), (bv_rd, 2),
                                         (bo_rd, 3)]:
                            dmx(g, rowA(i2), srcd[:], swB, 'wB')

                wB_c0 = X['wB']

                def emit_wo_phase(mt):
                    with nc.Block() as blk:
                        peS = C['pe']; dvS = C['dv']; ldS = C['ld']
                        alS = [X['al0'], X['al1']]

                        @blk.sync
                        def _(s, mt=mt, peS=peS, dvS=dvS):
                            s.wait_ge(swB, wB_c0)
                            s.wait_ge(sao, X['ao'])
                            for st in range(4):
                                if st >= 2:
                                    s.wait_ge(spe, peS + 2 * (st - 2) + 1)
                                p = st % 2
                                dmx(s, aoTM[:, p * 1024:(p + 1) * 1024],
                                    bass.AP(scr_ao, (mt * 512 + st * 128) * 1024,
                                            [[1024, 128], [1, 1024]]),
                                    sal[p], 'al%d' % p)
                            s.wait_ge(sdv, dvS + 8)
                            dmx(s, bass.AP(y1scr, mt * 512 * 1024,
                                           [[1024, 128], [131072, 4], [1, 1024]]),
                                XN[:, mt * 4096:(mt + 1) * 4096].rearrange(
                                    "p (s w) -> p s w", s=4), sy1, 'y1')

                        @blk.tensor
                        def _(t, peS=peS, dvS=dvS, alS=alS):
                            for st in range(4):
                                t.wait_ge(sal[st % 2],
                                          alS[st % 2] + 16 * (st // 2 + 1))
                                if st >= 1:
                                    t.wait_ge(sdv, dvS + 2 * st - 1)
                                i = None
                                for cc in range(8):
                                    i = t.transpose(
                                        pT0c[:, cc * 128:(cc + 1) * 128],
                                        aoTM[:, (st % 2) * 1024 + cc * 128:
                                             (st % 2) * 1024 + (cc + 1) * 128],
                                        cIDb[:])
                                fin(i, spe, 'pe')
                                t.wait_ge(sdv, dvS + 2 * st + 1)
                                if st >= 1:
                                    t.wait_ge(sdv, dvS + 2 * st)
                                i = None
                                for cc in range(8):
                                    for nh in range(2):
                                        i = t.matmul(
                                            pBig0[:, nh * 512:(nh + 1) * 512],
                                            aoT[:, cc * 128:(cc + 1) * 128],
                                            wob[:, cc * 1024 + nh * 512:
                                                cc * 1024 + (nh + 1) * 512],
                                            start=(cc == 0), stop=False)
                                for nh in range(2):
                                    i = t.matmul(pBig0[:, nh * 512:(nh + 1) * 512],
                                                 cONE[0:1, 0:128],
                                                 rowA(3)[0:1, nh * 512:(nh + 1) * 512],
                                                 start=False, stop=True)
                                fin(i, spe, 'pe')

                        @blk.vector
                        def _(v, mt=mt, peS=peS):
                            for st in range(4):
                                v.wait_ge(spe, peS + 2 * st + 1)
                                i = v.tensor_copy(aoT[:], pT0c[:, :])
                                finv(v, sdv, 'dv')
                                v.wait_ge(spe, peS + 2 * st + 2)
                                xsl = XN[:, (mt * 4 + st) * 1024:
                                         (mt * 4 + st + 1) * 1024]
                                i = v.tensor_add(xsl, xsl, pBig0[:, :])
                                finv(v, sdv, 'dv')

                def emit_a(mt):
                    with nc.Block() as blk:
                        peS = C['pe']; dvS = C['dv']; acS = C['ac']
                        s0S = X['s0']; s1S = X['s1']

                        @blk.vector
                        def _(v, mt=mt, peS=peS, acS=acS, s0S=s0S, s1S=s1S):
                            for st in range(4):
                                ln_stats(v, XN[:, (mt * 4 + st) * 1024:
                                               (mt * 4 + st + 1) * 1024], st)
                            ln_combine(v)
                            newton_rstd(v, ST[:, 16:20], ST[:, 20:24],
                                        ST[:, 36:40], 4)
                            for kind, st in (("a", 0), ("a", 1), ("c", 0),
                                             ("a", 2), ("c", 1), ("a", 3),
                                             ("c", 2), ("c", 3)):
                                if kind == "a":
                                    if st >= 2:
                                        v.wait_ge(spe, peS + st - 1)
                                    i = v.tensor_scalar(
                                        xhb[:, (st % 2) * 1024:
                                            (st % 2 + 1) * 1024],
                                        XN[:, (mt * 4 + st) * 1024:
                                           (mt * 4 + st + 1) * 1024],
                                        ST[:, 16 + st:17 + st],
                                        ST[:, 36 + st:37 + st],
                                        ALU.subtract, ALU.mult)
                                else:
                                    v.wait_ge(spe, peS + st + 1)
                                    i = v.tensor_copy(
                                        chunked(xhT, 8, 512)[
                                            :, :, st * 128:(st + 1) * 128],
                                        chunked(pT0c, 8, 128))
                                finv(v, sdv, 'dv')
                            T3 = tmb[:, 0:2048].rearrange(
                                "p (h x) -> p h x", h=16)
                            for st in range(4):
                                v.wait_ge(sac, acS + 3 * st + 1)
                                v.wait_ge(ss0, s0S + 16 * st)
                                i = v.scalar_tensor_tensor(
                                    T3[:, :, 0:64],
                                    pBig0[:, :].rearrange(
                                        "p (h e) -> p h e", h=16),
                                    1.0,
                                    JUNK[:].rearrange(
                                        "p (h e) -> p h e", h=16),
                                    ALU.add, ALU.max)
                                finv(v, sdv, 'dv')
                                v.wait_ge(sac, acS + 3 * st + 2)
                                i = v.scalar_tensor_tensor(
                                    T3[:, :, 64:128],
                                    pBig1[:, :].rearrange(
                                        "p (h e) -> p h e", h=16),
                                    1.0,
                                    JUNK[:].rearrange(
                                        "p (h e) -> p h e", h=16),
                                    ALU.add, ALU.max)
                                finv(v, sdv, 'dv')

                        @blk.tensor
                        def _(t, mt=mt, dvS=dvS, acS=acS):
                            t.wait_ge(swB, wB_c0)
                            A_inc = (1, 2, 4, 6)
                            C_inc = (3, 5, 7, 8)
                            for st in range(4):
                                t.wait_ge(sdv, dvS + A_inc[st])
                                if st >= 1:
                                    t.wait_ge(sdv, dvS + C_inc[st - 1])
                                i = None
                                for cc in range(8):
                                    i = t.transpose(
                                        pT0c[:, cc * 128:(cc + 1) * 128],
                                        xhb[:, (st % 2) * 1024 + cc * 128:
                                            (st % 2) * 1024 + (cc + 1) * 128],
                                        cIDb[:])
                                fin(i, spe, 'pe')
                            for st in range(4):
                                t.wait_ge(sdv, dvS + C_inc[st])
                                if st >= 1:
                                    t.wait_ge(sac, acS + 3 * st)
                                i = None
                                for cc in range(8):
                                    for nh in range(2):
                                        i = t.matmul(
                                            pBig0[:, nh * 512:(nh + 1) * 512],
                                            xhT[:, cc * 512 + st * 128:
                                                cc * 512 + (st + 1) * 128],
                                            wqb[:, cc * 1024 + nh * 512:
                                                cc * 1024 + (nh + 1) * 512],
                                            start=(cc == 0), stop=False)
                                for nh in range(2):
                                    i = t.matmul(pBig0[:, nh * 512:(nh + 1) * 512],
                                                 cONE[0:1, 0:128],
                                                 rowA(0)[0:1, nh * 512:(nh + 1) * 512],
                                                 start=False, stop=True)
                                fin(i, spe, 'pe')
                                if st >= 1:
                                    t.wait_ge(sdv, dvS + 8 + 2 * st)
                                i = None
                                for cc in range(8):
                                    for nh in range(2):
                                        i = t.matmul(
                                            pBig1[:, nh * 512:(nh + 1) * 512],
                                            xhT[:, cc * 512 + st * 128:
                                                cc * 512 + (st + 1) * 128],
                                            wkb[:, cc * 1024 + nh * 512:
                                                cc * 1024 + (nh + 1) * 512],
                                            start=(cc == 0), stop=False)
                                for nh in range(2):
                                    i = t.matmul(pBig1[:, nh * 512:(nh + 1) * 512],
                                                 cONE[0:1, 0:128],
                                                 rowA(1)[0:1, nh * 512:(nh + 1) * 512],
                                                 start=False, stop=True)
                                fin(i, spe, 'pe')
                                t.wait_ge(sdv, dvS + 9 + 2 * st)
                                i = None
                                for cc in range(8):
                                    for nh in range(2):
                                        i = t.matmul(
                                            pBig0[:, nh * 512:(nh + 1) * 512],
                                            xhT[:, cc * 512 + st * 128:
                                                cc * 512 + (st + 1) * 128],
                                            wvb[:, cc * 1024 + nh * 512:
                                                cc * 1024 + (nh + 1) * 512],
                                            start=(cc == 0), stop=False)
                                for nh in range(2):
                                    i = t.matmul(pBig0[:, nh * 512:(nh + 1) * 512],
                                                 cONE[0:1, 0:128],
                                                 rowA(2)[0:1, nh * 512:(nh + 1) * 512],
                                                 start=False, stop=True)
                                fin(i, spe, 'pe')

                        @blk.scalar
                        def _(a, dvS=dvS, peS=peS, s1S=s1S):
                            for st in range(4):
                                a.wait_ge(spe, peS + 5 + 3 * st)
                                if st >= 1:
                                    a.wait_ge(sdv, dvS + 8 + 2 * st)
                                i = a.activation(JUNK[:], pBig0[:, :], AF.Exp)
                                finv(a, sac, 'ac')
                                a.wait_ge(spe, peS + 6 + 3 * st)
                                a.wait_ge(sdv, dvS + 9 + 2 * st)
                                i = a.activation(JUNK[:], pBig1[:, :], AF.Exp)
                                finv(a, sac, 'ac')
                                a.wait_ge(spe, peS + 7 + 3 * st)
                                a.wait_ge(ss1, s1S + 16 * st)
                                i = a.activation(AMK[:, :], pBig0[:, :],
                                                 AF.Identity)
                                finv(a, sac, 'ac')

                        @blk.sync
                        def _(s, mt=mt, dvS=dvS, acS=acS, s0S=s0S, s1S=s1S):
                            s.wait_ge(swB, wB_c0)

                            def store_st(st):
                                rb = (mt * 8194 + st * 2048
                                      + (1 if st >= 2 else 0)) * 128
                                s.wait_ge(sdv, dvS + 10 + 2 * st)
                                dmx(s, bass.AP(scr_qk, rb,
                                               [[2048, 128], [1, 2048]]),
                                    tmb[:, 0:2048], ss0, 's0')
                                s.wait_ge(sac, acS + 3 * st + 3)
                                dmx(s, bass.AP(scr_v, (mt * 512 + st * 128) * 1024,
                                               [[1024, 128], [1, 1024]]),
                                    AMK[:, :], ss1, 's1')
                            store_st(0)
                            store_st(1)
                            s.wait_ge(ss0, s0S + 32)
                            s.wait_ge(ss1, s1S + 32)
                            h0 = mt * 8194 * 128
                            s.dma_start(
                                out=FQ2h[:],
                                in_=bass.AP(scr_qk, h0,
                                            [[128, 4096], [1, 128]]),
                                transpose=True).then_inc(sfq, 16)
                            X['fq'] += 16
                            s.dma_start(
                                out=FK2h[:],
                                in_=bass.AP(scr_qk, h0 + 64,
                                            [[128, 4096], [1, 128]]),
                                transpose=True).then_inc(sfq, 16)
                            X['fq'] += 16
                            dmx(s, V2h[:, :].rearrange("p (g e) -> p g e", g=32),
                                bass.AP(scr_v, mt * 512 * 1024,
                                        [[1024, 8], [64, 16], [8192, 32], [1, 64]]),
                                sfq, 'fq')
                            store_st(2)
                            store_st(3)

                def T_s1(b):
                    return b + 1 + (b // 2)

                def T_s2(B2):
                    return 3 * (B2 + 1)

                def V_m(b):
                    return b + 1 if b < 4 else b + 4

                def V_recip(r):
                    return 5 if r == 0 else 12

                def V_div(B2):
                    return (6, 7, 13, 14)[B2]

                def emit_b(mt):
                    with nc.Block() as blk:
                        peS = C['pe']; dvS = C['dv']
                        fqS = X['fq']; aoS = X['ao']
                        s0S = X['s0']; s1S = X['s1']

                        @blk.sync
                        def _(s, mt=mt, peS=peS, dvS=dvS, s0S=s0S, s1S=s1S):
                            s.wait_ge(swB, wB_c0)
                            s.wait_ge(ss0, s0S)
                            s.wait_ge(ss1, s1S)
                            s.wait_ge(spe, peS + 12)
                            h1 = (mt * 8194 + 4097) * 128
                            s.dma_start(
                                out=FQ2h[:],
                                in_=bass.AP(scr_qk, h1,
                                            [[128, 4096], [1, 128]]),
                                transpose=True).then_inc(sfq, 16)
                            X['fq'] += 16
                            s.dma_start(
                                out=FK2h[:],
                                in_=bass.AP(scr_qk, h1 + 64,
                                            [[128, 4096], [1, 128]]),
                                transpose=True).then_inc(sfq, 16)
                            X['fq'] += 16
                            dmx(s, V2h[:, :].rearrange("p (g e) -> p g e", g=32),
                                bass.AP(scr_v, mt * 512 * 1024 + 32 * 8192,
                                        [[1024, 8], [64, 16], [8192, 32], [1, 64]]),
                                sfq, 'fq')
                            for hf in range(2):
                                s.wait_ge(sdv, dvS + 14 * hf + 14)
                                dmx(s, bass.AP(scr_ao,
                                               mt * 512 * 1024 + hf * 32 * 8192,
                                               [[64, 128], [8192, 32], [1, 64]]),
                                    aog[:, :].rearrange("p (g e) -> p g e", g=32),
                                    sao, 'ao')

                        @blk.tensor
                        def _(t, dvS=dvS, peS=peS, fqS=fqS, aoS=aoS):
                            for hf in range(2):
                                t.wait_ge(sfq, fqS + 48 * hf)
                                dvH = dvS + 14 * hf
                                if hf == 1:
                                    t.wait_ge(sao, aoS + 16)
                                    t.wait_ge(sdv, dvS + 14)
                                for B2 in range(4):
                                    for bb in range(2):
                                        b = B2 * 2 + bb
                                        if b >= 2:
                                            t.wait_ge(sdv, dvH + V_m(b - 2))
                                        i = None
                                        for gl in range(4):
                                            g = b * 4 + gl
                                            i = t.matmul(
                                                pBig0[:, (b % 2) * 512 + gl * 128:
                                                      (b % 2) * 512 + (gl + 1) * 128],
                                                FK2h[0:64, g * 128:(g + 1) * 128],
                                                FQ2h[0:64, g * 128:(g + 1) * 128],
                                                start=True, stop=True)
                                        fin(i, spe, 'pe')
                                    t.wait_ge(sdv, dvH + V_m(2 * B2 + 1))
                                    if B2 >= 2:
                                        t.wait_ge(sdv, dvH + V_div(B2 - 2))
                                    i = None
                                    for gl in range(8):
                                        g = B2 * 8 + gl
                                        b_of_g = g // 4
                                        amsl = AMK[:, (b_of_g % 2) * 512
                                                   + (g % 4) * 128:
                                                   (b_of_g % 2) * 512
                                                   + (g % 4 + 1) * 128]
                                        pqk = pQKa if (g // 16) % 2 == 0 else pQKb
                                        t.matmul(
                                            pBig1[:, (B2 % 2) * 512 + gl * 64:
                                                  (B2 % 2) * 512 + (gl + 1) * 64],
                                            amsl, V2h[:, g * 64:(g + 1) * 64],
                                            start=True, stop=True)
                                        i = t.matmul(
                                            pqk[:, (g % 16):(g % 16) + 1],
                                            amsl, cCOL[:],
                                            start=True, stop=True)
                                    fin(i, spe, 'pe')

                        @blk.vector
                        def _(v, dvS=dvS, peS=peS, aoS=aoS, s1S=s1S,
                              ldNow=None):
                            v.wait_ge(sld, C['ld'])
                            v.wait_ge(ss1, s1S)
                            for hf in range(2):
                                peH = peS + 12 * hf
                                dvH = dvS + 14 * hf
                                for half_b in range(2):
                                    for b in range(half_b * 4, half_b * 4 + 4):
                                        v.wait_ge(spe, peH + T_s1(b))
                                        if b >= 2:
                                            v.wait_ge(spe,
                                                      peH + T_s2((b - 2) // 2))
                                        i = v.tensor_tensor(
                                            AMK[:, (b % 2) * 512:(b % 2 + 1) * 512],
                                            pBig0[:, (b % 2) * 512:(b % 2 + 1) * 512],
                                            cMSK[:], ALU.mult)
                                        finv(v, sdv, 'dv')
                                    r = half_b
                                    v.wait_ge(spe, peH + T_s2(2 * r + 1))
                                    pqk = pQKa if r == 0 else pQKb
                                    v.tensor_scalar(qkr[:, r * 16:(r + 1) * 16],
                                                    pqk[:, :], 1e-6, None,
                                                    ALU.add)
                                    v.drain()
                                    i = v.reciprocal(qkr[:, r * 16:(r + 1) * 16],
                                                     qkr[:, r * 16:(r + 1) * 16])
                                    finv(v, sdv, 'dv')
                                    for B2 in (2 * r, 2 * r + 1):
                                        if hf == 1 and B2 == 0:
                                            v.wait_ge(sao, aoS + 16)
                                        i = v.tensor_tensor(
                                            aog[:, B2 * 512:(B2 + 1) * 512]
                                            .rearrange("p (a e) -> p a e", a=8),
                                            pBig1[:, (B2 % 2) * 512:
                                                  (B2 % 2 + 1) * 512]
                                            .rearrange("p (a e) -> p a e", a=8),
                                            qkr[:, r * 16 + (B2 % 2) * 8:
                                                r * 16 + (B2 % 2) * 8 + 8]
                                            .unsqueeze(2).broadcast_to(
                                                [128, 8, 64]),
                                            ALU.mult)
                                        finv(v, sdv, 'dv')

                for mt in range(4):
                    if mt >= 1:
                        emit_wo_phase(mt - 1)
                    emit_a(mt)
                    emit_b(mt)
                emit_wo_phase(3)

        with ExitStack() as exD:
            ED = exD.enter_context
            cIDf = ED(nc.sbuf_tensor("cIDf", [128, 128], F32))
            fcwb = ED(nc.sbuf_tensor("fcwb", [128, 32768], BF16))
            pjcb = ED(nc.sbuf_tensor("pjcb", [128, 8192], BF16))
            h2b = ED(nc.sbuf_tensor("h2b", [128, 4096], BF16))
            h2T = ED(nc.sbuf_tensor("h2T", [128, 4096], BF16))
            g1F = ED(nc.sbuf_tensor("g1F", [128, 16384], BF16))
            y1t = ED(nc.sbuf_tensor("y1t", [128, 8192], F32))
            sFd = ED(nc.sbuf_tensor("sFd", [128, 1024], F32))
            xre = ED(nc.sbuf_tensor("xre", [128, 4096], F32))
            gsc = ED(nc.sbuf_tensor("gsc", [128, 2048], F32))
            pT0d = ED(nc.psum_tensor("pT0d", [128, 1024], BF16))
            pMa = ED(nc.psum_tensor("pMa", [128, 512], F32))
            pMb = ED(nc.psum_tensor("pMb", [128, 512], F32))
            pPa = ED(nc.psum_tensor("pPa", [128, 512], F32))
            pPb = ED(nc.psum_tensor("pPb", [128, 512], F32))
            pTf = ED(nc.psum_tensor("pTf", [128, 512], F32))
            pS4a = ED(nc.psum_tensor("pS4a", [128, 512], F32))
            pS4b = ED(nc.psum_tensor("pS4b", [128, 512], F32))

            y1v = lambda mt: y1t[:, (mt % 2) * 4096:(mt % 2 + 1) * 4096]

            with nc.Block() as blk:
                stD0 = C['st']

                @blk.gpsimd
                def _(g):
                    g.wait_ge(sy1, X['y1'])
                    for q in range(4):
                        if q >= 2:
                            g.wait_ge(sfc[q % 2], X['fc%d' % (q % 2)])
                        for cc in range(8):
                            srcap = bass.AP(fcw_d.tensor,
                                            cc * 128 * 4096 + q * 1024,
                                            [[4096, 128], [128, 8], [1, 128]])
                            dstap = bass.AP(fcwb, q * 8 * 1024 + cc * 128,
                                            [[32768, 128], [1024, 8], [1, 128]])
                            g.dma_start(out=dstap,
                                        in_=srcap).then_inc(sfc[q % 2], 16)
                            X['fc%d' % (q % 2)] += 16

                @blk.sync
                def _(s):
                    s.wait_ge(sy1, X['y1'])
                    dma(s, cIDf[:], identf_d[:], sld, 'ld')
                    dmx(s, y1v(0),
                        bass.AP(y1scr, 0, [[1024, 128], [131072, 4], [1, 1024]]),
                        sly[0], 'ly0')

            fcB = [X['fc0'] - 256, X['fc1'] - 256]

            def load_pj(g, dout, slot):
                srcap = bass.AP(pjw_d.tensor, dout * 128,
                                [[1024, 128], [128 * 1024, 32], [1, 128]])
                dstap = pjcb[:, slot * 4096:(slot + 1) * 4096].rearrange(
                    "p (m w) -> p m w", m=32)
                g.dma_start(out=dstap, in_=srcap).then_inc(spj[slot], 16)
                X['pj%d' % slot] += 16

            dv0 = C['dv']; pe0 = C['pe']; ac0 = C['ac']; ldD = C['ld']
            pj0B = [X['pj0'], X['pj1']]
            skb = [X['k0'], X['k1'], X['k2'], X['k3']]
            sobb = [X['ob0'], X['ob1'], X['ob2'], X['ob3']]
            A_base = (0, 4, 24, 44)
            D_base = (8, 28, 48, 64)
            G_base = (16, 36, 56, 72)
            f_pe = lambda mt: pe0 + 60 * mt
            f_ac = lambda mt: ac0 + 80 * mt
            pjX = lambda mt, sl: pj0B[sl] + 64 * mt

            with nc.Block() as blk:
                @blk.sync
                def _(s):
                    for mt in range(4):
                        if mt + 1 < 4:
                            p = (mt + 1) % 2
                            if mt >= 1:
                                s.wait_ge(sdv, dv0 + G_base[mt - 1] + 8)
                            dmx(s, y1v(mt + 1),
                                bass.AP(y1scr, (mt + 1) * 512 * 1024,
                                        [[1024, 128], [131072, 4], [1, 1024]]),
                                sly[p], 'ly%d' % p)
                        for j in range(4):
                            k = 4 * mt + j
                            if k >= 4:
                                s.wait_ge(sob[j],
                                          sobb[j] + 16 * (k // 4))
                            dmx(s, xre[:, j * 1024:(j + 1) * 1024],
                                x8k[k * 128:(k + 1) * 128, :],
                                sk_[j], 'k%d' % j)
                        for j in range(4):
                            k = 4 * mt + j
                            s.wait_ge(sdv, dv0 + G_base[mt] + 2 * j + 2)
                            dmx(s, out_d[k * 128:(k + 1) * 128, :],
                                xre[:, j * 1024:(j + 1) * 1024],
                                sob[j], 'ob%d' % j)
                    for j in range(4):
                        s.wait_ge(sob[j], X['ob%d' % j])
                    s.wait_ge(sou, C['ou'])

                @blk.gpsimd
                def _(g):
                    for mt in range(4):
                        for d in range(8):
                            if d >= 2:
                                g.wait_ge(spe, f_pe(mt) + 33 + 2 * d)
                            elif mt >= 1:
                                g.wait_ge(spe, f_pe(mt - 1) + 49 + 2 * d)
                            load_pj(g, d, d % 2)

                @blk.vector
                def _(v):
                    def emit_ln(mt):
                        v.wait_ge(sly[mt % 2], 16 * (mt // 2 + 1))
                        if mt == 0:
                            v.wait_ge(sld, ldD)
                        for st in range(4):
                            ln_stats(v, y1v(mt)[:, st * 1024:
                                                (st + 1) * 1024], st)
                        ln_combine(v)
                        newton_rstd(v, ST[:, 16:20], ST[:, 20:24],
                                    ST[:, 36:40], 4)
                        for st in range(4):
                            if mt >= 1:
                                v.wait_ge(spe, f_pe(mt - 1) + st + 1)
                            i = v.tensor_scalar(
                                h2b[:, st * 1024:(st + 1) * 1024],
                                y1v(mt)[:, st * 1024:(st + 1) * 1024],
                                ST[:, 16 + st:17 + st], ST[:, 36 + st:37 + st],
                                ALU.subtract, ALU.mult)
                            finv(v, sdv, 'dv')

                    def emit_adds(mt):
                        for d in range(8):
                            v.wait_ge(spe, f_pe(mt) + 38 + 2 * d)
                            i = v.tensor_tensor(
                                y1v(mt).rearrange(
                                    "p (s w) -> p s w",
                                    s=4)[:, :, d * 128:(d + 1) * 128],
                                pTf[:, :].rearrange("p (s w) -> p s w", s=4),
                                y1v(mt).rearrange(
                                    "p (s w) -> p s w",
                                    s=4)[:, :, d * 128:(d + 1) * 128],
                                ALU.add)
                            finv(v, sdv, 'dv')
                        for j in range(4):
                            k = 4 * mt + j
                            gs = gsc[:, (k % 2) * 1024:(k % 2 + 1) * 1024]
                            v.wait_ge(sac, f_ac(mt) + 72 + 2 * j + 2)
                            i = v.tensor_tensor(
                                gs, gs,
                                y1v(mt)[:, j * 1024:(j + 1) * 1024],
                                ALU.mult)
                            finv(v, sdv, 'dv')
                            v.wait_ge(sk_[j],
                                      skb[j] + 16 * (k // 4 + 1))
                            i = v.tensor_add(
                                xre[:, j * 1024:(j + 1) * 1024],
                                xre[:, j * 1024:(j + 1) * 1024], gs)
                            finv(v, sdv, 'dv')

                    emit_ln(0)
                    emit_ln(1)
                    emit_adds(0)
                    emit_ln(2)
                    emit_adds(1)
                    emit_ln(3)
                    emit_adds(2)
                    emit_adds(3)

                @blk.tensor
                def _(t):
                    t.wait_ge(sld, ldD)
                    for mt in range(4):
                        for st in range(4):
                            t.wait_ge(sdv, dv0 + A_base[mt] + st + 1)
                            if st >= 1:
                                t.wait_ge(sac, f_ac(mt) + 8 * st)
                            elif mt >= 1:
                                t.wait_ge(sac, f_ac(mt - 1) + 32)
                            i = None
                            for cc in range(8):
                                i = t.transpose(
                                    pT0d[:, cc * 128:(cc + 1) * 128],
                                    h2b[:, st * 1024 + cc * 128:
                                        st * 1024 + (cc + 1) * 128],
                                    cIDb[:])
                            fin(i, spe, 'pe')
                        t.wait_ge(sac, f_ac(mt) + 32)
                        for m in range(32):
                            q = m // 8
                            t.wait_ge(sfc[q % 2],
                                      fcB[q % 2] + 128 * (q // 2 + 1))
                            if m >= 2:
                                t.wait_ge(sac, f_ac(mt) + 32 + m - 1)
                            dstp = pMa if m % 2 == 0 else pMb
                            i = None
                            for cc in range(8):
                                i = t.matmul(dstp[:, :],
                                             fcwb[:, m * 1024 + cc * 128:
                                                  m * 1024 + (cc + 1) * 128],
                                             h2T[:, cc * 512:(cc + 1) * 512],
                                             start=(cc == 0), stop=(cc == 7))
                            fin(i, spe, 'pe')
                        t.wait_ge(sac, f_ac(mt) + 64)
                        for d in range(8):
                            t.wait_ge(spj[d % 2],
                                      pjX(mt, d % 2) + 16 * (d // 2 + 1))
                            if d >= 2:
                                t.wait_ge(sac, f_ac(mt) + 64 + d - 1)
                            elif mt >= 1:
                                t.wait_ge(sac, f_ac(mt - 1) + 71 + d)
                            dstp = pPa if d % 2 == 0 else pPb
                            i = None
                            for m in range(32):
                                i = t.matmul(
                                    dstp[:, :],
                                    pjcb[:, (d % 2) * 4096 + m * 128:
                                         (d % 2) * 4096 + (m + 1) * 128],
                                    g1F[:, m * 512:(m + 1) * 512],
                                    start=(m == 0), stop=(m == 31))
                            fin(i, spe, 'pe')
                            t.wait_ge(sac, f_ac(mt) + 64 + d + 1)
                            if d >= 1:
                                t.wait_ge(sdv, dv0 + D_base[mt] + d)
                            elif mt >= 1:
                                t.wait_ge(sdv, dv0 + D_base[mt - 1] + 8)
                            i = None
                            for j in range(4):
                                i = t.transpose(
                                    pTf[:, j * 128:(j + 1) * 128],
                                    sFd[:, (d % 2) * 512 + j * 128:
                                        (d % 2) * 512 + (j + 1) * 128],
                                    cIDf[:])
                            fin(i, spe, 'pe')
                        for j in range(4):
                            k = 4 * mt + j
                            for nh in range(2):
                                if k >= 1:
                                    km, jm = (k - 1) // 4, (k - 1) % 4
                                    t.wait_ge(sac, f_ac(km) + 72 + 2 * jm
                                              + nh + 1)
                                dstp = pS4a if nh == 0 else pS4b
                                i = t.matmul(
                                    dstp[:, :],
                                    ALPHAT[:, k * 128:(k + 1) * 128],
                                    SV2[:, nh * 512:(nh + 1) * 512],
                                    start=True, stop=True)
                                fin(i, spe, 'pe')

                @blk.scalar
                def _(a):
                    for mt in range(4):
                        for st in range(4):
                            a.wait_ge(spe, f_pe(mt) + st + 1)
                            i = None
                            for cc in range(8):
                                i = a.activation(
                                    h2T[:, cc * 512 + st * 128:
                                        cc * 512 + (st + 1) * 128],
                                    pT0d[:, cc * 128:(cc + 1) * 128],
                                    AF.Identity, bias=LNC[:, 24 + cc:25 + cc],
                                    scale=LNC[:, 16 + cc:17 + cc])
                                finv(a, sac, 'ac')
                        for m in range(32):
                            a.wait_ge(spe, f_pe(mt) + 4 + m + 1)
                            srcp = pMa if m % 2 == 0 else pMb
                            i = a.activation(g1F[:, m * 512:(m + 1) * 512],
                                             srcp[:, :], AF.Gelu,
                                             bias=BC[:, m:m + 1])
                            finv(a, sac, 'ac')
                        for d in range(8):
                            a.wait_ge(spe, f_pe(mt) + 37 + 2 * d)
                            if d >= 2:
                                a.wait_ge(spe, f_pe(mt) + 34 + 2 * d)
                            srcp = pPa if d % 2 == 0 else pPb
                            i = a.activation(sFd[:, (d % 2) * 512:
                                                 (d % 2 + 1) * 512],
                                             srcp[:, :], AF.Identity,
                                             bias=BC[:, 32 + d:33 + d])
                            finv(a, sac, 'ac')
                        for j in range(4):
                            k = 4 * mt + j
                            for nh in range(2):
                                a.wait_ge(spe, f_pe(mt) + 52 + 2 * j
                                          + nh + 1)
                                if k >= 2:
                                    m2, j2 = (k - 2) // 4, (k - 2) % 4
                                    a.wait_ge(sdv, dv0 + G_base[m2]
                                              + 2 * j2 + 2)
                                srcp = pS4a if nh == 0 else pS4b
                                i = a.activation(
                                    gsc[:, (k % 2) * 1024 + nh * 512:
                                        (k % 2) * 1024 + (nh + 1) * 512],
                                    srcp[:, :], AF.Silu)
                                finv(a, sac, 'ac')

            assert C['dv'] == dv0 + 80, (C['dv'], dv0 + 80)
            assert C['pe'] == f_pe(4), (C['pe'], f_pe(4))
            assert C['ac'] == f_ac(4), (C['ac'], f_ac(4))

    return nc


def kernel(**inputs):
    nc = build_nc(dbg=False)
    in_maps = [make_core_inputs(c, inputs) for c in range(NCORES)]
    res = run_bass_kernel_spmd(nc, in_maps, core_ids=list(range(NCORES)))
    out = np.empty((2, T_B, D), np.float32)
    for c in range(NCORES):
        b, off = c // 4, (c % 4) * T_OWN
        out[b, off:off + T_OWN] = np.asarray(res.results[c]["out"])
    return out
